# version 02
# baseline (speedup 1.0000x reference)
import os
import numpy as np
import ml_dtypes
from contextlib import ExitStack

import concourse.bass as bass
import concourse.mybir as mybir
from concourse.alu_op_type import AluOpType as OP
from concourse.bass_utils import run_bass_kernel_spmd

F32 = mybir.dt.float32
BF16 = mybir.dt.bfloat16
AF = mybir.ActivationFunctionType
AX = mybir.AxisListType.X

D = 1024
N = 2048
NB = 8
CH = 512
NCH = N // CH
HG = 4
DVH = 512
DVG = 2048
HF = 16
EXP = 8
LNQ = float(np.log(1.0 / 16.0))

BUILD_DEBUG = False
P1_CUT = 99
BAR_LN1 = False
FFN_CUT = 99
GLA_CHUNKS = NCH
FFN_QT = 16
MOE_EXP = EXP


class SemC:
    __slots__ = ("h", "n")

    def __init__(self, h):
        self.h = h
        self.n = 0


class SemView:
    __slots__ = ("core", "base")

    def __init__(self, core, base):
        self.core = core
        self.base = base

    @property
    def h(self):
        return self.core.h

    @property
    def n(self):
        return self.core.n - self.base

    @n.setter
    def n(self, v):
        self.core.n = self.base + v


class K:
    ENGINES = ("tensor", "vector", "scalar", "gpsimd", "sync")

    def __init__(self, nc, es):
        self.nc = nc
        self.es = es
        self.ops = []
        self.pinned = {}
        self._pool = []
        self._map = {}
        self._next = 0
        self.dummy = {}

    def recycle(self):
        self._map = {}
        self._next = 0

    def sem(self, name, pin=False):
        if pin:
            if name not in self.pinned:
                self.pinned[name] = SemC(self.es.enter_context(self.nc.semaphore(name)))
            return self.pinned[name]
        if name in self.pinned:
            return self.pinned[name]
        if name not in self._map:
            if self._next < len(self._pool):
                s = self._pool[self._next]
            else:
                s = SemC(self.es.enter_context(self.nc.semaphore(f"pool{self._next}")))
                self._pool.append(s)
            self._next += 1
            self._map[name] = SemView(s, s.n)
        return self._map[name]

    def op(self, eng, fn, wait=(), inc=()):
        w = [(s.h, v + (s.base if isinstance(s, SemView) else 0)) for (s, v) in wait]
        i = []
        for (s, kk) in inc:
            s.n += kk
            i.append((s.h, kk))
        if os.environ.get("KAUDIT") and fn.__closure__:
            snap = [(nm, c.cell_contents) for nm, c in
                    zip(fn.__code__.co_freevars, fn.__closure__)]
            self.ops.append((eng, fn, w, i, snap))
        else:
            self.ops.append((eng, fn, w, i, None))

    def bar(self):
        self.ops.append("BAR")

    def flush(self):
        segs = [[]]
        for o in self.ops:
            if o == "BAR":
                segs.append([])
            else:
                segs[-1].append(o)
        self.ops = []
        if os.environ.get("KDBG"):
            self._fl = getattr(self, "_fl", 0) + 1
            for si, seg in enumerate(segs):
                cnt = {}
                for o in seg:
                    cnt[o[0]] = cnt.get(o[0], 0) + 1
                if cnt:
                    print(f"FLUSH{self._fl} seg{si}: {cnt}")
                for o in seg:
                    if o[0] == "sync":
                        print(f"  sync w={o[2]} i={o[3]}")
        for seg in segs:
            if not seg:
                continue
            with self.nc.Block() as blk:
                for eng in self.ENGINES:
                    mine = [o for o in seg if o[0] == eng]
                    if not mine:
                        continue

                    def body(e, mine=mine, eng=eng):
                        dmy = self.dummy.get(eng)
                        for (_, fn, w, i, snap) in mine:
                            if snap is not None:
                                for (nm, v0), c in zip(snap, fn.__closure__):
                                    v1 = c.cell_contents
                                    same = v1 is v0
                                    if not same:
                                        try:
                                            same = bool(v1 == v0)
                                        except Exception:
                                            same = False
                                    if not same:
                                        raise RuntimeError(
                                            f"LATE-BINDING: freevar '{nm}' changed "
                                            f"between record and flush in {eng} op")
                            for (h, v) in w:
                                e.wait_ge(h, v)
                            inst = fn(e)
                            for j, (h, kk) in enumerate(i):
                                if inst is None or j > 0:
                                    if dmy is None:
                                        e.nop().then_inc(h, kk)
                                    else:
                                        dmy(e).then_inc(h, kk)
                                else:
                                    inst.then_inc(h, kk)
                    getattr(blk, eng)(body)

    def mm(self, psum, pairs, wait=(), inc=()):
        n = len(pairs)
        for j, (lt, rh) in enumerate(pairs):
            self.op("tensor",
                    lambda e, o=psum, lt=lt, rh=rh, st=(j == 0), sp=(j == n - 1):
                    e.matmul(o, lt, rh, start=st, stop=sp),
                    wait=wait if j == 0 else (), inc=inc if j == n - 1 else ())


class PP:
    def __init__(self, k, name, nbuf, pinc=1, cinc=1, pin=False):
        self.prod = k.sem(name + "_p", pin=pin)
        self.cons = k.sem(name + "_c", pin=pin)
        self.pb = self.prod.n
        self.cb = self.cons.n
        self.nbuf = nbuf
        self.pinc = pinc
        self.cinc = cinc
        self.rw = 0
        self.rr = 0

    def w_wait(self):
        if self.rw < self.nbuf:
            return []
        return [(self.cons, self.cb + (self.rw - self.nbuf + 1) * self.cinc)]

    def w_inc(self):
        self.rw += 1
        return [(self.prod, self.pinc)]

    def r_wait(self):
        return [(self.prod, self.pb + (self.rr + 1) * self.pinc)]

    def r_inc(self):
        self.rr += 1
        return [(self.cons, self.cinc)]

    @property
    def widx(self):
        return self.rw % self.nbuf

    @property
    def ridx(self):
        return self.rr % self.nbuf


def _rope_tables():
    half = 32
    inv = 1.0 / 10000.0 ** (np.arange(half, dtype=np.float64) / half)
    ang = np.arange(N, dtype=np.float64)[:, None] * inv[None, :]
    cos = np.cos(ang)
    sin = np.sin(ang)
    cosP = np.zeros((N, D), np.float64)
    sinP = np.zeros((N, D), np.float64)
    for i in range(half):
        cosP[:, 2 * i::64] = cos[:, i:i + 1]
        cosP[:, 2 * i + 1::64] = cos[:, i:i + 1]
        sinP[:, 2 * i::64] = -sin[:, i:i + 1]
        sinP[:, 2 * i + 1::64] = sin[:, i:i + 1]
    return cosP.astype(np.float32), sinP.astype(np.float32)


def _rope_perm():
    perm = np.zeros(D, np.int64)
    for h in range(HF):
        for i in range(32):
            perm[h * 64 + 2 * i] = h * 64 + i
            perm[h * 64 + 2 * i + 1] = h * 64 + 32 + i
    return perm


def _host_prep(inp):
    c = np.ascontiguousarray
    f = np.float32
    d = {}
    d["WqT"] = c(np.asarray(inp["L0_attn_Wq"], f).T)
    d["WkT"] = c(np.asarray(inp["L0_attn_Wk"], f).T)
    d["WvT"] = c(np.asarray(inp["L0_attn_Wv"], f).T)
    d["WgT"] = c(np.asarray(inp["L0_attn_Wg"], f).T)
    d["Wgk1T"] = c(np.asarray(inp["L0_attn_Wgk1"], f).T)
    d["Wgk2b"] = c(np.concatenate([np.asarray(inp["L0_attn_Wgk2"], f).T,
                                   np.asarray(inp["L0_attn_bgk2"], f)[None, :]], 0))
    d["WoT"] = c(np.asarray(inp["L0_attn_Wout"], f).T)
    d["rmsw_row"] = c(np.asarray(inp["L0_attn_rms_w"], f)[None, :])
    perm = _rope_perm()
    d["wq_r"] = c(np.asarray(inp["L0_ffn_wq"], f)[:, perm])
    d["wk_r"] = c(np.asarray(inp["L0_ffn_wk"], f)[:, perm])
    d["wv"] = c(np.asarray(inp["L0_ffn_wv"], f))
    d["wo"] = c(np.asarray(inp["L0_ffn_wo"], f))
    d["s1col"] = c(np.asarray(inp["L0_ffn_scale1"], f).reshape(8, 128).T)
    d["s2col"] = c(np.asarray(inp["L0_ffn_scale2"], f).reshape(8, 128).T)
    d["router_c"] = c(np.asarray(inp["L0_ffn_router_w"], f).reshape(8, 128, 8)
                      .transpose(1, 0, 2).reshape(128, 64))
    d["w1"] = c(np.asarray(inp["L0_ffn_w1"], f).reshape(EXP * D, D))
    d["w2"] = c(np.asarray(inp["L0_ffn_w2"], f).reshape(EXP * D, D))
    d["wout"] = c(np.asarray(inp["L0_ffn_wout"], f).reshape(EXP * D, D))
    d["b1c"] = c(np.asarray(inp["L0_ffn_b1"], f).T.reshape(8, 128, 8)
                 .transpose(1, 0, 2).reshape(128, 64))
    d["b2c"] = c(np.asarray(inp["L0_ffn_b2"], f).T.reshape(8, 128, 8)
                 .transpose(1, 0, 2).reshape(128, 64))
    d["bout"] = c(np.asarray(inp["L0_ffn_bout"], f))
    d["ident"] = np.eye(128, dtype=f)
    cosP, sinP = _rope_tables()
    d["cosP"] = cosP
    d["sinP"] = sinP
    maskA = np.zeros((128, 4, 512), f)
    stair = np.zeros((128, 4, 512), f)
    p = np.arange(128)[:, None]
    t = np.arange(512)[None, :]
    for r in range(4):
        maskA[:, r, :] = (t >= p + r * 128).astype(f)
        stair[:, r, :] = np.where(t <= r * 128 + p, 0.0, -1e9).astype(f)
    d["maskA"] = c(maskA.reshape(128, 2048))
    d["stairS"] = c(stair.reshape(128, 2048))
    import jax
    with jax.default_device(jax.devices("cpu")[0]):
        noise = np.asarray(jax.random.gumbel(jax.random.key(42), (NB, N, EXP)), f) * f(0.05)
    d["_noise_all"] = noise
    return d


def build(dbg=False, phases=3):
    nc = bass.Bass("TRN2", target_bir_lowering=False)
    g = {}
    for nm, sh in [("xT", (D, N)), ("WqT", (D, D)), ("WkT", (D, D)), ("WvT", (D, DVG)),
                   ("WgT", (D, DVG)), ("Wgk1T", (D, 32)), ("Wgk2b", (33, D)), ("WoT", (DVG, D)),
                   ("rmsw_row", (1, 512)),
                   ("wq_r", (D, D)), ("wk_r", (D, D)), ("wv", (D, D)), ("wo", (D, D)),
                   ("s1col", (128, 8)), ("s2col", (128, 8)), ("router_c", (128, 64)),
                   ("w1", (EXP * D, D)), ("w2", (EXP * D, D)), ("wout", (EXP * D, D)),
                   ("b1c", (128, 64)), ("b2c", (128, 64)), ("bout", (EXP, D)),
                   ("ident", (128, 128)), ("cosP", (N, D)), ("sinP", (N, D)),
                   ("maskA", (128, 2048)), ("stairS", (128, 2048)), ("noise", (128, 128)),
                   ("onehot8", (8, 1024))]:
        g[nm] = nc.dram_tensor(nm, list(sh), F32, kind="ExternalInput").ap()
    out = nc.dram_tensor("out", [D, N], F32, kind="ExternalOutput").ap()
    r2_scr = nc.dram_tensor("r2_scr", [D, N], BF16, kind="Internal").ap()
    q_scr = nc.dram_tensor("q_scr", [N, D], BF16, kind="Internal").ap()
    kT_scr = nc.dram_tensor("kT_scr", [D, N], BF16, kind="Internal").ap()
    v_scr = nc.dram_tensor("v_scr", [N, D], BF16, kind="Internal").ap()
    dbg_out = {}
    if dbg:
        for nm, sh, dt in [("d_ln1", (128, 4096), BF16), ("d_x1", (128, 8 * N), F32),
                           ("d_kT", (128, 8 * N), BF16), ("d_s", (128, 8 * N), F32),
                           ("d_u", (8, N), BF16), ("d_r2", (128, 8 * N), BF16),
                           ("d_qe", (128, 1024), BF16), ("d_kd", (128, 1024), BF16),
                           ("d_vc", (128, 4 * DVG), BF16), ("d_S", (128, 8 * 512), F32),
                           ("d_vsb", (128, 16 * D), BF16), ("d_x1b", (128, 8 * N), BF16),
                           ("d_hh", (128, 8 * 128), F32), ("d_e1", (128, 1024), BF16),
                           ("d_cl", (128, 1024), F32), ("d_e2", (128, 1024), BF16),
                           ("d_rowm", (1, N), F32), ("d_rowa", (1, N), F32),
                           ("d_bA", (128, N), BF16), ("d_bB", (128, N), BF16)]:
            dbg_out[nm] = nc.dram_tensor(nm, list(sh), dt, kind="ExternalOutput").ap()

    es_g = ExitStack()
    with es_g:
        k = K(nc, es_g)

        def sbt(name, p, f_, dt, es=es_g):
            return es.enter_context(nc.sbuf_tensor(name, [p, f_], dt))

        X = sbt("X", 128, 8 * N, F32)
        identb = sbt("identb", 128, 128, BF16)
        identf = sbt("identf", 128, 128, F32)
        ones_col = sbt("ones_col", 128, 1, BF16)
        ones_rb = sbt("ones_rb", 1, 128, BF16)
        ones_rf = sbt("ones_rf", 1, 128, F32)
        s1c = sbt("s1c", 128, 8, F32)
        s1cb = sbt("s1cb", 128, 8, BF16)
        s2c = sbt("s2c", 128, 8, F32)
        cb = sbt("cb", 128, 4, F32)

        dumt = sbt("dumt", 1, 8, F32)
        k.dummy["tensor"] = lambda e: e.ldweights(identb[:, 0:128])
        k.dummy["vector"] = lambda e: e.tensor_copy(dumt[0:1, 0:1], dumt[0:1, 1:2])
        k.dummy["scalar"] = lambda e: e.copy(dumt[0:1, 2:3], dumt[0:1, 3:4])
        k.dummy["gpsimd"] = lambda e: e.tensor_copy(dumt[0:1, 4:5], dumt[0:1, 5:6])

        dma = PP(k, "dma_in", 10 ** 9, pinc=16, pin=True)
        dmag = PP(k, "dma_ing", 10 ** 9, pinc=16, pin=True)

        def load(dst, src, eng="sync"):
            k.op(eng, lambda e, d=dst, s=src: e.dma_start(d, s),
                 inc=(dma.w_inc() if eng == "sync" else dmag.w_inc()))

        def dpair():
            return [(dma.prod, dma.prod.n), (dmag.prod, dmag.prod.n)]

        load(X[:, :].rearrange("p (kt n) -> p kt n", n=N),
             g["xT"].rearrange("(kt p) n -> p kt n", p=128))
        load(identb[:, :], g["ident"], "gpsimd")
        load(identf[:, :], g["ident"])
        load(s1c[:, :], g["s1col"])
        load(s1cb[:, :], g["s1col"], "gpsimd")
        load(s2c[:, :], g["s2col"])
        k.op("vector", lambda e: e.memset(ones_col[:, :], 1.0))
        k.op("vector", lambda e: e.memset(ones_rb[:, :], 1.0))
        k.op("vector", lambda e: e.memset(ones_rf[:, :], 1.0))
        k.op("vector", lambda e: e.memset(cb[:, 0:1], 1.0))
        k.op("vector", lambda e: e.memset(cb[:, 1:2], 1e-5))
        k.op("vector", lambda e: e.memset(cb[:, 2:3], 1e-6))
        k.op("vector", lambda e: e.memset(cb[:, 3:4], LNQ))
        setup_pair = dpair()
        k.bar()

        G = dict(X=X, identb=identb, identf=identf, ones_col=ones_col,
                 ones_rb=ones_rb, ones_rf=ones_rf, s1c=s1c, s1cb=s1cb, s2c=s2c, cb=cb,
                 dpair=dpair, dmag=dmag)

        def emit_out():
            outd = k.sem("final_outd", pin=True)
            for dm in range(8):
                k.op("sync", lambda e, dm=dm: e.dma_start(
                    out[dm * 128:(dm + 1) * 128, :], X[:, dm * N:(dm + 1) * N]),
                     inc=[(outd, 16)])
            k.op("sync", lambda e: e.nop(), wait=[(outd, outd.n)])
            k.flush()

        _p1_gla(k, nc, g, G, dma, load, setup_pair, dbg, dbg_out)
        if phases < 2:
            emit_out()
            return nc, dbg_out
        k.recycle()
        _ffn(k, nc, g, G, dma, load, dbg, dbg_out, q_scr, r2_scr, kT_scr, v_scr)
        if phases < 3:
            emit_out()
            return nc, dbg_out
        k.recycle()
        _moe(k, nc, g, G, dma, load, dbg, dbg_out, r2_scr, out)

    return nc, dbg_out


def _p1_gla(k, nc, g, G, dma, load, setup_pair, dbg, dbg_out):
    cb = G["cb"]
    X = G["X"]
    identb = G["identb"]
    ones_col = G["ones_col"]
    ones_rf = G["ones_rf"]
    ones_rb = G["ones_rb"]
    es1 = ExitStack()
    with es1:
        sb1 = lambda n_, p, f_, dt: es1.enter_context(nc.sbuf_tensor(n_, [p, f_], dt))
        ps1 = lambda n_, p, f_, dt: es1.enter_context(nc.psum_tensor(n_, [p, f_], dt))

        bA = sb1("bA", 128, N, BF16)
        bB = sb1("bB", 128, N, BF16)
        rmsw_b = sb1("rmsw_b", 128, 512, BF16)
        rmsw_row_s = sb1("rmsw_row_s", 1, 512, BF16)

        pBig = ps1("pBig", 128, 2048, F32)
        pDec = ps1("pDec", 128, 512, F32)
        pT = ps1("pT", 128, 1024, BF16)
        pS = ps1("pS", 128, 512, F32)

        load(rmsw_row_s[:, :], g["rmsw_row"], "gpsimd")

        es_p1 = ExitStack()
        with es_p1:
            rowA = es_p1.enter_context(nc.sbuf_tensor("rowA", [1, N], F32))
            rowB = es_p1.enter_context(nc.sbuf_tensor("rowB", [1, N], F32))
            rowT = es_p1.enter_context(nc.sbuf_tensor("rowT", [1, N], F32))
            xsq = es_p1.enter_context(nc.sbuf_tensor("xsq", [128, 2 * N], BF16))

            mmg = k.sem("mmg")
            sqd = k.sem("sqd")

            def stats_pass(out_row, func, extra_first=()):
                for tt in range(4):
                    for kt in range(8):
                        w = []
                        if kt == 0:
                            w = [(mmg, mmg.n)] + (list(extra_first) if tt == 0 else [])
                        if func is None:
                            k.op("scalar", lambda e, kt=kt, tt=tt: e.copy(
                                xsq[:, kt * 512:(kt + 1) * 512],
                                X[:, kt * N + tt * 512: kt * N + tt * 512 + 512]),
                                 wait=w, inc=[(sqd, 1)])
                        else:
                            k.op("scalar", lambda e, kt=kt, tt=tt, func=func: e.activation(
                                xsq[:, kt * 512:(kt + 1) * 512],
                                X[:, kt * N + tt * 512: kt * N + tt * 512 + 512], func),
                                 wait=w, inc=[(sqd, 1)])
                    k.mm(pBig[0:1, tt * 512: tt * 512 + 512],
                         [(ones_col[:, :], xsq[:, kt * 512:(kt + 1) * 512]) for kt in range(8)],
                         wait=[(sqd, sqd.n)], inc=[(mmg, 1)])
                ev = k.sem("stats_ev" + str(mmg.n))
                for tt in range(4):
                    k.op("vector", lambda e, tt=tt, out_row=out_row: e.tensor_scalar(
                        out_row[:, tt * 512: tt * 512 + 512], pBig[0:1, tt * 512: tt * 512 + 512],
                        1.0 / D, None, OP.mult), wait=[(mmg, mmg.n)], inc=[(ev, 1)])
                return ev

            sxe = stats_pass(rowA, None, extra_first=setup_pair)
            sxe_t = sxe.n
            if dbg:
                k.op("sync", lambda e: e.dma_start(dbg_out["d_rowm"], rowA[:, :]),
                     wait=[(sxe, sxe_t)], inc=dma.w_inc())
                k.op("vector", lambda e: e.nop(), wait=G["dpair"]())
                k.op("scalar", lambda e: e.nop(), wait=G["dpair"]())
            ve2 = stats_pass(rowB, AF.Square, extra_first=[(sxe, sxe_t)])
            if dbg:
                k.op("sync", lambda e: e.dma_start(dbg_out["d_rowa"], rowB[:, :]),
                     wait=[(ve2, ve2.n)], inc=dma.w_inc())
                k.op("vector", lambda e: e.nop(), wait=G["dpair"]())
                k.op("scalar", lambda e: e.nop(), wait=G["dpair"]())

            vrow = k.sem("vrow")
            srow = k.sem("srow")
            k.op("vector", lambda e: e.tensor_tensor(rowT[:, :], rowA[:, :], rowA[:, :], OP.mult),
                 wait=[(ve2, ve2.n)])
            k.op("vector", lambda e: e.tensor_tensor(rowB[:, :], rowB[:, :], rowT[:, :], OP.subtract),
                 inc=[(vrow, 1)])
            k.op("scalar", lambda e: e.activation(rowB[:, :], rowB[:, :], AF.Ln, bias=cb[0:1, 1:2]),
                 wait=[(vrow, 1)])
            k.op("scalar", lambda e: e.activation(rowT[:, :], rowB[:, :], AF.Exp, scale=-0.5),
                 inc=[(srow, 1)])
            k.op("vector", lambda e: e.tensor_tensor(rowA[:, :], rowA[:, :], rowT[:, :], OP.mult),
                 wait=[(srow, 1)], inc=[(vrow, 1)])
            if P1_CUT <= 3:
                k.bar(); k.flush(); return
            bcd = k.sem("bcd")
            bcm = k.sem("bcm")
            for tt in range(4):
                k.op("tensor", lambda e, tt=tt: e.matmul(
                    pBig[:, 0:512], ones_rf[:, :], rowT[:, tt * 512: tt * 512 + 512],
                    start=True, stop=True),
                     wait=[(vrow, 2)] if tt == 0 else [(bcd, 2 * tt)])
                k.op("tensor", lambda e, tt=tt: e.matmul(
                    pBig[:, 512:1024], ones_rf[:, :], rowA[:, tt * 512: tt * 512 + 512],
                    start=True, stop=True), inc=[(bcm, 1)])
                k.op("scalar", lambda e, tt=tt: e.copy(bA[:, tt * 512: tt * 512 + 512], pBig[:, 0:512]),
                     wait=[(bcm, tt + 1)])
                k.op("scalar", lambda e, tt=tt: e.activation(
                    bB[:, tt * 512: tt * 512 + 512], pBig[:, 512:1024], AF.Identity, scale=-1.0),
                     inc=[(bcd, 2)])
            if P1_CUT <= 4:
                k.bar(); k.flush(); return
            if dbg:
                k.op("sync", lambda e: e.dma_start(dbg_out["d_bA"], bA[:, :]),
                     wait=[(bcd, 8)], inc=dma.w_inc())
                k.op("sync", lambda e: e.dma_start(dbg_out["d_bB"], bB[:, :]),
                     wait=[(bcd, 8)], inc=dma.w_inc())
                k.op("vector", lambda e: e.nop(), wait=G["dpair"]())
            rwb = k.sem("rwb")
            k.op("tensor", lambda e: e.matmul(pBig[:, 0:512], ones_rb[:, :], rmsw_row_s[:, :],
                                              start=True, stop=True),
                 wait=G["dpair"]() + [(bcd, 8)], inc=[(rwb, 1)])
            k.op("scalar", lambda e: e.copy(rmsw_b[:, :], pBig[:, 0:512]), wait=[(rwb, 1)],
                 inc=[(rwb, 1)])
            if P1_CUT <= 5:
                k.bar(); k.flush(); return
            k.bar()
            k.flush()

        onesf = sb1("onesf", 128, CH, F32)
        ln1_c = sb1("ln1_c", 128, 8 * 512, BF16)
        z1aug_c = sb1("z1aug_c", 33, 512, BF16)
        Wgk1T_s = sb1("Wgk1T_s", 128, 8 * 32, BF16)
        Wgk2b_s = sb1("Wgk2b_s", 33, D, BF16)
        maskA_s = sb1("maskA_s", 128, 2048, BF16)
        arena = sb1("arena", 128, 2 * 4096, BF16)
        S_sb = sb1("S_sb", 128, 8 * 512, F32)
        S_bf = sb1("S_bf", 128, 2 * 512, BF16)
        qe_h = sb1("qe_h", 128, 2 * 512, BF16)
        kd_h = sb1("kd_h", 128, 2 * 512, BF16)
        e2_h = sb1("e2_h", 128, 2 * 512, BF16)
        v_c = sb1("v_c", 128, 4 * DVG, BF16)
        g_c = sb1("g_c", 128, 4 * DVG, BF16)
        A_sb = sb1("A_sb", 128, 4 * CH, BF16)
        kdec = sb1("kdec", 128, 4 * 256, BF16)
        z_Th = sb1("z_Th", 128, 4 * 512, BF16)
        e3col = sb1("e3col", 128, 8, F32)
        escr = sb1("escr", 128, 2 * 512, F32)
        lscr = sb1("lscr", 128, 2 * 512, F32)
        cumL = sb1("cumL", 128, 2 * 512, F32)
        e1scr = sb1("e1scr", 128, 2 * 512, BF16)
        osb = sb1("osb", 128, 2 * DVH, BF16)
        sqz = sb1("sqz", 128, 2 * DVH, BF16)
        zout = sb1("zout", 128, 2 * DVH, BF16)
        ssq = sb1("ssq", 128, 16, F32)
        rstd = sb1("rstd", 128, 16, F32)
        xbp = sb1("xbp", 128, 2 * 512, BF16)
        vscr = sb1("vscr", 128, 512, BF16)

        load(Wgk1T_s[:, :].rearrange("p (kt c) -> p kt c", c=32),
             g["Wgk1T"].rearrange("(kt p) c -> p kt c", p=128), "gpsimd")
        load(Wgk2b_s[:, :], g["Wgk2b"], "gpsimd")
        load(maskA_s[:, :], g["maskA"], "gpsimd")
        gla_pair = G["dpair"]()
        k.op("vector", lambda e: e.memset(onesf[:, :], 1.0))
        k.op("vector", lambda e: e.memset(z1aug_c[32:33, :], 1.0))

        aw = PP(k, "aw", 2, pinc=16)
        pqk = PP(k, "pqk", 2)
        pa = PP(k, "pa", 1)
        po = PP(k, "po", 1)
        pt_pp = PP(k, "pt_pp", 1)
        ps_pp = PP(k, "ps_pp", 1)
        pdec = PP(k, "pdec", 1)
        dec_sc = PP(k, "dec_sc", 2)
        dec_cum = PP(k, "dec_cum", 2)
        dec_e = PP(k, "dec_e", 2)
        lnp = PP(k, "lnp", 2)
        e3r = k.sem("e3r")
        zdone = k.sem("zdone")
        ztr = k.sem("ztr")
        xgla = k.sem("xgla", pin=True)
        otile = 0

        def aslot():
            return arena[:, aw.widx * 4096: aw.widx * 4096 + 4096]

        def arena_round(src3d, kt_n, c_n, extra_wait=()):
            sl = aslot()[:, 0:kt_n * c_n]
            d3 = sl.rearrange("p (kt c) -> p kt c", c=c_n)
            k.op("gpsimd", lambda e, d=d3, s=src3d: e.dma_start(d, s),
                 wait=aw.w_wait() + list(extra_wait), inc=aw.w_inc())
            return sl

        for c in range(GLA_CHUNKS):
            lnr = k.sem("lnr")
            for kt in range(8):
                i = kt % 2
                k.op("scalar", lambda e, kt=kt, c=c, i=i: e.copy(
                    xbp[:, i * 512:(i + 1) * 512], X[:, kt * N + c * CH: kt * N + c * CH + CH]),
                     wait=lnp.w_wait(), inc=lnp.w_inc())
                k.op("vector", lambda e, i=i, c=c: e.tensor_tensor(
                    vscr[:, :], xbp[:, i * 512:(i + 1) * 512],
                    bA[:, c * 512: c * 512 + 512], OP.mult),
                     wait=lnp.r_wait() + ([(bcd, 8)] if (c == 0 and kt == 0) else []))
                k.op("vector", lambda e, kt=kt, c=c: e.tensor_tensor(
                    ln1_c[:, kt * 512:(kt + 1) * 512], vscr[:, :],
                    bB[:, c * 512: c * 512 + 512], OP.add),
                     inc=lnp.r_inc() + [(lnr, 1)])
            ln_t = lnr.n
            if BAR_LN1:
                k.bar()
            if dbg and c == 0:
                k.op("sync", lambda e: e.dma_start(dbg_out["d_ln1"], ln1_c[:, :]),
                     wait=[(lnr, ln_t)], inc=dma.w_inc())
                k.op("vector", lambda e: e.nop(), wait=G["dpair"]())
                k.op("scalar", lambda e: e.nop(), wait=G["dpair"]())
            z1d = k.sem("z1d")
            k.mm(pDec[0:32, 0:512],
                 [(Wgk1T_s[:, kt * 32:(kt + 1) * 32], ln1_c[:, kt * 512:(kt + 1) * 512])
                  for kt in range(8)],
                 wait=pdec.w_wait() + [(lnr, ln_t)] + (gla_pair if c == 0 else []),
                 inc=pdec.w_inc())
            k.op("scalar", lambda e: e.copy(z1aug_c[0:32, :], pDec[0:32, 0:512]),
                 wait=pdec.r_wait(), inc=pdec.r_inc() + [(z1d, 1)])

            vgev = k.sem("vgev")
            for wname, dstbuf, silu in [("WvT", v_c, False), ("WgT", g_c, True)]:
                for dvs in range(4):
                    sl = arena_round(
                        g[wname][:, dvs * 512:(dvs + 1) * 512].rearrange("(kt p) c -> p kt c", p=128),
                        8, 512)
                    asl_t = aw.prod.n
                    for tp in range(4):
                        wslot = pBig[:, pqk.widx * 512: pqk.widx * 512 + 512]
                        w = pqk.w_wait() + ([(aw.prod, asl_t)] if tp == 0 else [])
                        if c == 0 and wname == "WvT" and dvs == 0 and tp == 0:
                            w = w + [(rwb, 2)]
                        k.mm(wslot, [(ln1_c[:, kt * 512 + tp * 128: kt * 512 + tp * 128 + 128],
                                      sl[:, kt * 512:(kt + 1) * 512]) for kt in range(8)],
                             wait=w + ([(lnr, ln_t)] if (dvs == 0 and tp == 0) else []),
                             inc=pqk.w_inc())
                        if silu:
                            k.op("scalar", lambda e, tp=tp, dvs=dvs, sl2=wslot, dstbuf=dstbuf: e.activation(
                                dstbuf[:, tp * DVG + dvs * 512: tp * DVG + dvs * 512 + 512], sl2, AF.Silu),
                                 wait=pqk.r_wait(), inc=pqk.r_inc() + [(vgev, 1)])
                        else:
                            k.op("scalar", lambda e, tp=tp, dvs=dvs, sl2=wslot, dstbuf=dstbuf: e.copy(
                                dstbuf[:, tp * DVG + dvs * 512: tp * DVG + dvs * 512 + 512], sl2),
                                 wait=pqk.r_wait(), inc=pqk.r_inc() + [(vgev, 1)])
                    aw.rr += 1
                    k.op("tensor", lambda e: e.nop(), inc=[(aw.cons, 1)])
            vg_t = vgev.n
            if dbg and c == 0:
                k.op("sync", lambda e: e.dma_start(dbg_out["d_vc"], v_c[:, :]),
                     wait=[(vgev, vg_t)], inc=dma.w_inc())

            for h in range(HG):
                slq = arena_round(
                    g["WqT"][:, h * 256:(h + 1) * 256].rearrange("(kt p) c -> p kt c", p=128),
                    8, 256)
                aq_t = aw.prod.n
                qk_ev = k.sem("qk_ev")
                for dkt in range(2):
                    pt = 2 * h + dkt
                    s2_ = slice(dkt * 512, dkt * 512 + 512)
                    k.op("tensor", lambda e, pt=pt, c=c: e.matmul(
                        pDec[:, 0:512], Wgk2b_s[:, pt * 128:(pt + 1) * 128],
                        z1aug_c[:, :], start=True, stop=True),
                         wait=pdec.w_wait() + ([(z1d, z1d.n)] if (h == 0 and dkt == 0) else []),
                         inc=pdec.w_inc())
                    k.op("scalar", lambda e, s2_=s2_: e.activation(
                        escr[:, s2_], pDec[:, 0:512], AF.Exp, scale=-1.0),
                         wait=pdec.r_wait() + dec_sc.w_wait(), inc=pdec.r_inc())
                    k.op("scalar", lambda e, s2_=s2_: e.activation(
                        lscr[:, s2_], escr[:, s2_], AF.Ln, bias=cb[:, 0:1]), inc=dec_sc.w_inc())
                    k.op("vector", lambda e, s2_=s2_: e.tensor_tensor_scan(
                        cumL[:, s2_], onesf[:, :], lscr[:, s2_], 0.0, OP.mult, OP.add),
                         wait=dec_sc.r_wait() + dec_cum.w_wait(),
                         inc=dec_sc.r_inc() + dec_cum.w_inc())
                    k.op("scalar", lambda e, s2_=s2_: e.activation(
                        e1scr[:, s2_], cumL[:, s2_], AF.Exp, scale=-1.0 / 32.0, bias=cb[:, 3:4]),
                         wait=dec_cum.r_wait() + dec_e.w_wait())
                    k.op("scalar", lambda e, s2_=s2_: e.activation(
                        e2_h[:, s2_], cumL[:, s2_], AF.Exp, scale=1.0 / 32.0))
                    k.op("scalar", lambda e, pt=pt, s2_=s2_: e.activation(
                        e3col[:, pt:pt + 1], cumL[:, s2_.stop - 1: s2_.stop], AF.Exp,
                        scale=-1.0 / 32.0),
                         inc=dec_cum.r_inc() + dec_e.w_inc() + [(e3r, 1)])
                    e_tgt = dec_e.prod.n
                    wslot = pBig[:, pqk.widx * 512: pqk.widx * 512 + 512]
                    k.mm(wslot, [(slq[:, kt * 256 + dkt * 128: kt * 256 + dkt * 128 + 128],
                                  ln1_c[:, kt * 512:(kt + 1) * 512]) for kt in range(8)],
                         wait=pqk.w_wait() + ([(aw.prod, aq_t), (lnr, ln_t)] if dkt == 0 else []),
                         inc=pqk.w_inc())
                    k.op("vector", lambda e, s2_=s2_, sl2=wslot: e.tensor_tensor(
                        qe_h[:, s2_], sl2, e1scr[:, s2_], OP.mult),
                         wait=pqk.r_wait() + [(dec_e.prod, e_tgt)] + ([(po.prod, po.prod.n)] if dkt == 0 else []),
                         inc=pqk.r_inc() + [(qk_ev, 1)])
                aw.rr += 1
                k.op("tensor", lambda e: e.nop(), inc=[(aw.cons, 1)])
                slk = arena_round(
                    g["WkT"][:, h * 256:(h + 1) * 256].rearrange("(kt p) c -> p kt c", p=128),
                    8, 256)
                ak_t = aw.prod.n
                for dkt in range(2):
                    s2_ = slice(dkt * 512, dkt * 512 + 512)
                    wslot = pBig[:, pqk.widx * 512: pqk.widx * 512 + 512]
                    k.mm(wslot, [(slk[:, kt * 256 + dkt * 128: kt * 256 + dkt * 128 + 128],
                                  ln1_c[:, kt * 512:(kt + 1) * 512]) for kt in range(8)],
                         wait=pqk.w_wait() + ([(aw.prod, ak_t)] if dkt == 0 else []),
                         inc=pqk.w_inc())
                    k.op("vector", lambda e, s2_=s2_, sl2=wslot: e.tensor_tensor(
                        kd_h[:, s2_], sl2, e2_h[:, s2_], OP.mult),
                         wait=pqk.r_wait() + ([(pa.prod, pa.prod.n)] if dkt == 0 else []),
                         inc=pqk.r_inc() + [(qk_ev, 1)] + dec_e.r_inc())
                aw.rr += 1
                k.op("tensor", lambda e: e.nop(), inc=[(aw.cons, 1)])
                qk_t = qk_ev.n
                if dbg and c == 0 and h == 0:
                    k.op("sync", lambda e: e.dma_start(dbg_out["d_qe"], qe_h[:, :]),
                         wait=[(qk_ev, qk_t)], inc=dma.w_inc())
                    k.op("sync", lambda e: e.dma_start(dbg_out["d_e1"], e1scr[:, :]),
                         wait=[(qk_ev, qk_t)], inc=dma.w_inc())
                    k.op("sync", lambda e: e.dma_start(dbg_out["d_e2"], e2_h[:, :]),
                         wait=[(qk_ev, qk_t)], inc=dma.w_inc())
                    k.op("sync", lambda e: e.dma_start(dbg_out["d_cl"], cumL[:, :]),
                         wait=[(qk_ev, qk_t)], inc=dma.w_inc())
                    k.op("vector", lambda e: e.nop(), wait=G["dpair"]())
                    k.op("scalar", lambda e: e.nop(), wait=G["dpair"]())
                    k.op("sync", lambda e: e.dma_start(dbg_out["d_kd"], kd_h[:, :]),
                         wait=[(qk_ev, qk_t)], inc=dma.w_inc())
                sbf = k.sem("sbf")
                if c > 0:
                    for i2 in range(2):
                        k.op("scalar", lambda e, h=h, i2=i2: e.copy(
                            S_bf[:, i2 * 512:(i2 + 1) * 512],
                            S_sb[:, (2 * h + i2) * 512:(2 * h + i2 + 1) * 512]),
                             wait=[(po.prod, po.prod.n)] if i2 == 0 else [], inc=[(sbf, 1)])
                sbf_t = sbf.n
                aev = k.sem("aev")
                for jt in range(4):
                    k.mm(pBig[:, 1024:1536],
                         [(kd_h[:, i2 * 512 + jt * 128: i2 * 512 + jt * 128 + 128],
                           qe_h[:, i2 * 512:(i2 + 1) * 512]) for i2 in range(2)],
                         wait=pa.w_wait() + ([(qk_ev, qk_t)] if jt == 0 else []),
                         inc=pa.w_inc())
                    k.op("vector", lambda e, jt=jt: e.tensor_tensor(
                        A_sb[:, jt * CH:(jt + 1) * CH], pBig[:, 1024:1536],
                        maskA_s[:, jt * 512:(jt + 1) * 512], OP.mult),
                         wait=pa.r_wait() + ([(po.prod, po.prod.n)] if jt == 0 else []),
                         inc=pa.r_inc() + [(aev, 1)])
                aev_t = aev.n
                kdv = k.sem("kdv")
                for i2 in range(2):
                    k.op("vector", lambda e, h=h, i2=i2: e.tensor_scalar(
                        sqz[:, i2 * 512:(i2 + 1) * 512], kd_h[:, i2 * 512:(i2 + 1) * 512],
                        e3col[:, 2 * h + i2: 2 * h + i2 + 1], None, OP.mult),
                         wait=[(e3r, e3r.n), (zdone, otile)] if i2 == 0 else [],
                         inc=[(kdv, 1)])
                kdv_t = kdv.n
                for i2 in range(2):
                    for jt in range(4):
                        first = (i2 == 0 and jt == 0)
                        last = (i2 == 1 and jt == 3)
                        k.op("tensor", lambda e, i2=i2, jt=jt: e.transpose(
                            pT[:, (i2 * 4 + jt) * 128:(i2 * 4 + jt) * 128 + 128],
                            sqz[:, i2 * 512 + jt * 128: i2 * 512 + jt * 128 + 128], identb[:, :]),
                             wait=(pt_pp.w_wait() + [(kdv, kdv_t)]) if first else [],
                             inc=pt_pp.w_inc() if last else [])
                kdt = k.sem("kdt")
                for i2 in range(2):
                    for jt in range(4):
                        first = (i2 == 0 and jt == 0)
                        last = (i2 == 1 and jt == 3)
                        k.op("scalar", lambda e, i2=i2, jt=jt: e.copy(
                            kdec[:, jt * 256 + i2 * 128: jt * 256 + i2 * 128 + 128],
                            pT[:, (i2 * 4 + jt) * 128:(i2 * 4 + jt) * 128 + 128]),
                             wait=pt_pp.r_wait() if first else [],
                             inc=(pt_pp.r_inc() + [(kdt, 1)]) if last else [])
                kdt_t = kdt.n
                ozs = k.sem("ozs")
                zTev = k.sem("zTev")
                for tt in range(4):
                    ib = otile % 2
                    si = otile % 16
                    pairs = []
                    if c > 0:
                        pairs += [(qe_h[:, i2 * 512 + tt * 128: i2 * 512 + tt * 128 + 128],
                                   S_bf[:, i2 * 512:(i2 + 1) * 512]) for i2 in range(2)]
                    pairs += [(A_sb[:, jt * CH + tt * 128: jt * CH + tt * 128 + 128],
                               v_c[:, jt * DVG + h * DVH: jt * DVG + h * DVH + DVH])
                              for jt in range(4)]
                    w = po.w_wait()
                    if tt == 0:
                        w = w + [(aev, aev_t), (vgev, vg_t)]
                        if c > 0:
                            w = w + [(sbf, sbf_t)]
                    k.mm(pBig[:, 1536:2048], pairs, wait=w, inc=po.w_inc())
                    k.op("scalar", lambda e, ib=ib: e.copy(osb[:, ib * 512:(ib + 1) * 512],
                                                           pBig[:, 1536:2048]),
                         wait=po.r_wait() + [(zdone, max(0, otile - 1))])
                    k.op("scalar", lambda e, ib=ib, si=si: e.activation(
                        sqz[:, ib * 512:(ib + 1) * 512], pBig[:, 1536:2048], AF.Square,
                        accum_out=ssq[:, si:si + 1]), inc=po.r_inc())
                    k.op("scalar", lambda e, si=si: e.activation(
                        rstd[:, si:si + 1], ssq[:, si:si + 1], AF.Ln, scale=1.0 / DVH, bias=cb[:, 1:2]))
                    k.op("scalar", lambda e, si=si: e.activation(
                        rstd[:, si:si + 1], rstd[:, si:si + 1], AF.Exp, scale=-0.5), inc=[(ozs, 1)])
                    k.op("vector", lambda e, ib=ib, h=h, tt=tt: e.tensor_tensor(
                        zout[:, ib * 512:(ib + 1) * 512], osb[:, ib * 512:(ib + 1) * 512],
                        g_c[:, tt * DVG + h * DVH: tt * DVG + h * DVH + DVH], OP.mult),
                         wait=[(ozs, ozs.n), (ztr, max(0, otile - 1))])
                    k.op("vector", lambda e, ib=ib: e.tensor_tensor(
                        zout[:, ib * 512:(ib + 1) * 512], zout[:, ib * 512:(ib + 1) * 512],
                        rmsw_b[:, :], OP.mult))
                    k.op("vector", lambda e, ib=ib, si=si: e.tensor_scalar(
                        zout[:, ib * 512:(ib + 1) * 512], zout[:, ib * 512:(ib + 1) * 512],
                        rstd[:, si:si + 1], None, OP.mult), inc=[(zdone, 1)])
                    zd_t = zdone.n
                    for dvs in range(4):
                        k.op("tensor", lambda e, ib=ib, dvs=dvs: e.transpose(
                            pT[:, dvs * 128: dvs * 128 + 128],
                            zout[:, ib * 512 + dvs * 128: ib * 512 + dvs * 128 + 128], identb[:, :]),
                             wait=(pt_pp.w_wait() + [(zdone, zd_t)]) if dvs == 0 else [],
                             inc=(pt_pp.w_inc() + [(ztr, 1)]) if dvs == 3 else [])
                    for dvs in range(4):
                        k.op("scalar", lambda e, dvs=dvs, tt=tt: e.copy(
                            z_Th[:, dvs * 512 + tt * 128: dvs * 512 + tt * 128 + 128],
                            pT[:, dvs * 128: dvs * 128 + 128]),
                             wait=pt_pp.r_wait() if dvs == 0 else [],
                             inc=(pt_pp.r_inc() + [(zTev, 1)]) if dvs == 3 else [])
                    otile += 1
                for i2 in range(2):
                    k.mm(pS[:, 0:512],
                         [(kdec[:, jt * 256 + i2 * 128: jt * 256 + i2 * 128 + 128],
                           v_c[:, jt * DVG + h * DVH: jt * DVG + h * DVH + DVH]) for jt in range(4)],
                         wait=ps_pp.w_wait() + ([(kdt, kdt_t)] if i2 == 0 else []),
                         inc=ps_pp.w_inc())
                    if c == 0:
                        k.op("vector", lambda e, h=h, i2=i2: e.tensor_copy(
                            S_sb[:, (2 * h + i2) * 512:(2 * h + i2 + 1) * 512], pS[:, 0:512]),
                             wait=ps_pp.r_wait(), inc=ps_pp.r_inc())
                    else:
                        k.op("vector", lambda e, h=h, i2=i2: e.scalar_tensor_tensor(
                            S_sb[:, (2 * h + i2) * 512:(2 * h + i2 + 1) * 512],
                            S_sb[:, (2 * h + i2) * 512:(2 * h + i2 + 1) * 512],
                            e3col[:, 2 * h + i2: 2 * h + i2 + 1], pS[:, 0:512], OP.mult, OP.add),
                             wait=ps_pp.r_wait(), inc=ps_pp.r_inc())
                slw = arena_round(
                    g["WoT"][h * 512:(h + 1) * 512, :].rearrange("(kt p) c -> p kt c", p=128),
                    4, 1024)
                awo_t = aw.prod.n
                zT_t = zTev.n
                for dm in range(8):
                    wslot = pBig[:, pqk.widx * 512: pqk.widx * 512 + 512]
                    w = pqk.w_wait() + ([(zTev, zT_t), (aw.prod, awo_t)] if dm == 0 else [])
                    k.mm(wslot, [(slw[:, kt * 1024 + dm * 128: kt * 1024 + dm * 128 + 128],
                                  z_Th[:, kt * 512:(kt + 1) * 512]) for kt in range(4)],
                         wait=w, inc=pqk.w_inc())
                    k.op("vector", lambda e, dm=dm, c=c, sl2=wslot: e.scalar_tensor_tensor(
                        X[:, dm * N + c * CH: dm * N + c * CH + CH], sl2, 1.0,
                        X[:, dm * N + c * CH: dm * N + c * CH + CH], OP.mult, OP.add),
                         wait=pqk.r_wait(), inc=pqk.r_inc() + [(xgla, 1)])
                aw.rr += 1
                k.op("tensor", lambda e: e.nop(), inc=[(aw.cons, 1)])
            if dbg and c == 0:
                k.op("vector", lambda e: e.nop(), wait=G["dpair"]())
            k.bar()

        if dbg:
            k.op("sync", lambda e: e.dma_start(dbg_out["d_x1"], X[:, :]),
                 wait=[(xgla, xgla.n)], inc=dma.w_inc())
            k.op("sync", lambda e: e.dma_start(dbg_out["d_S"], S_sb[:, :]), inc=dma.w_inc())
            k.op("vector", lambda e: e.nop(), wait=G["dpair"]())
        k.bar()
        k.flush()


def _ffn(k, nc, g, G, dma, load, dbg, dbg_out, q_scr, r2_scr, kT_scr, v_scr):
    cb = G["cb"]
    X = G["X"]
    identb, identf = G["identb"], G["identf"]
    ones_col, ones_rb, ones_rf = G["ones_col"], G["ones_rb"], G["ones_rf"]
    s1c, s1cb, s2c = G["s1c"], G["s1cb"], G["s2c"]
    es2 = ExitStack()
    with es2:
        sb2 = lambda n_, p, f_, dt: es2.enter_context(nc.sbuf_tensor(n_, [p, f_], dt))
        ps2 = lambda n_, p, f_, dt: es2.enter_context(nc.psum_tensor(n_, [p, f_], dt))

        warena = sb2("warena", 128, 8192, BF16)
        fscr = sb2("fscr", 128, 4096, BF16)
        bA2 = sb2("bA2", 128, N, BF16)
        bB2 = sb2("bB2", 128, N, BF16)
        betac = sb2("betac", 128, 16, F32)
        mxden = sb2("mxden", 128, 16, F32)
        Gb = sb2("Gb", 128, 128, BF16)
        grow = sb2("grow", 1, 128, F32)

        pSc = ps2("pSc", 128, 2048, F32)
        pTb = ps2("pTb", 128, 1024, BF16)
        pOT = ps2("pOT", 128, 512, F32)
        pYQ = ps2("pYQ", 128, 1024, F32)

        aw2 = k.sem("aw2")
        for kt in range(8):
            k.op("gpsimd", lambda e, kt=kt: e.dma_start(
                warena[:, kt * 1024:(kt + 1) * 1024], g["wk_r"][kt * 128:(kt + 1) * 128, :]),
                 inc=[(aw2, 16)])
        awk_t = aw2.n

        es2a = ExitStack()
        with es2a:
            sb2a = lambda n_, p, f_, dt: es2a.enter_context(nc.sbuf_tensor(n_, [p, f_], dt))
            x1b = sb2a("x1b", 128, 8 * N, BF16)
            rowA = sb2a("rowA2", 1, N, F32)
            rowB = sb2a("rowB2", 1, N, F32)
            rowT = sb2a("rowT2", 1, N, F32)
            m2row = sb2a("m2row", 1, N, BF16)
            negc = sb2a("negc", 1, 3 * D, BF16)
            tabring = sb2a("tabring", 128, 2 * 2048, BF16)
            stg = sb2a("stg", 128, 2 * 1024, BF16)

            mmg2 = k.sem("mmg2")
            sqd2 = k.sem("sqd2")

            xg = k.sem("xgla", pin=True)
            xg_t = xg.n

            def stats_pass2(out_row, func, extra_first=()):
                for tt in range(4):
                    for kt in range(8):
                        w = []
                        if kt == 0:
                            w = [(mmg2, mmg2.n)] + (list(extra_first) if tt == 0 else [])
                        if kt == 0 and tt == 0:
                            w = w + [(xg, xg_t)]
                        if func is None:
                            k.op("scalar", lambda e, kt=kt, tt=tt: e.copy(
                                fscr[:, kt * 512:(kt + 1) * 512],
                                X[:, kt * N + tt * 512: kt * N + tt * 512 + 512]),
                                 wait=w, inc=[(sqd2, 1)])
                        else:
                            k.op("scalar", lambda e, kt=kt, tt=tt, func=func: e.activation(
                                fscr[:, kt * 512:(kt + 1) * 512],
                                X[:, kt * N + tt * 512: kt * N + tt * 512 + 512], func),
                                 wait=w, inc=[(sqd2, 1)])
                    k.mm(pSc[0:1, tt * 512: tt * 512 + 512],
                         [(ones_col[:, :], fscr[:, kt * 512:(kt + 1) * 512]) for kt in range(8)],
                         wait=[(sqd2, sqd2.n)], inc=[(mmg2, 1)])
                ev = k.sem("stats2_ev" + str(mmg2.n))
                for tt in range(4):
                    k.op("vector", lambda e, tt=tt, out_row=out_row: e.tensor_scalar(
                        out_row[:, tt * 512: tt * 512 + 512], pSc[0:1, tt * 512: tt * 512 + 512],
                        1.0 / D, None, OP.mult), wait=[(mmg2, mmg2.n)], inc=[(ev, 1)])
                return ev

            if FFN_CUT <= 0:
                k.bar(); k.flush(); return
            sxe2 = stats_pass2(rowA, None)
            sxe2_t = sxe2.n
            ve22 = stats_pass2(rowB, AF.Square, extra_first=[(sxe2, sxe2_t)])
            mmq2 = mmg2

            if FFN_CUT <= 1:
                k.bar(); k.flush(); return
            vr2 = k.sem("vr2")
            sr2 = k.sem("sr2")
            k.op("vector", lambda e: e.tensor_tensor(rowT[:, :], rowA[:, :], rowA[:, :], OP.mult),
                 wait=[(ve22, ve22.n)])
            k.op("vector", lambda e: e.tensor_tensor(rowB[:, :], rowB[:, :], rowT[:, :], OP.subtract),
                 inc=[(vr2, 1)])
            k.op("scalar", lambda e: e.activation(rowT[:, :], rowB[:, :], AF.Ln, bias=cb[0:1, 1:2]),
                 wait=[(vr2, 1)])
            k.op("scalar", lambda e: e.activation(rowT[:, :], rowT[:, :], AF.Exp, scale=-0.5),
                 inc=[(sr2, 1)])
            k.op("vector", lambda e: e.tensor_tensor(rowB[:, :], rowB[:, :], rowT[:, :], OP.mult),
                 wait=[(sr2, 1)])
            k.op("vector", lambda e: e.tensor_tensor(rowB[:, :], rowB[:, :], rowT[:, :], OP.mult),
                 inc=[(vr2, 1)])
            k.op("scalar", lambda e: e.activation(rowB[:, :], rowB[:, :], AF.Ln, bias=cb[0:1, 2:3]),
                 wait=[(vr2, 2)])
            k.op("scalar", lambda e: e.activation(rowB[:, :], rowB[:, :], AF.Exp, scale=-0.5),
                 inc=[(sr2, 1)])
            k.op("vector", lambda e: e.tensor_tensor(rowB[:, :], rowB[:, :], rowT[:, :], OP.mult),
                 wait=[(sr2, 2)], inc=[(vr2, 1)])
            if FFN_CUT <= 2:
                k.bar(); k.flush(); return
            bcd2 = k.sem("bcd2")
            bcm2 = k.sem("bcm2")
            for tt in range(4):
                k.op("tensor", lambda e, tt=tt: e.matmul(
                    pYQ[:, 0:512], ones_rf[:, :], rowT[:, tt * 512: tt * 512 + 512],
                    start=True, stop=True),
                     wait=[(vr2, 3)] if tt == 0 else [(bcd2, 2 * tt)])
                k.op("tensor", lambda e, tt=tt: e.matmul(
                    pYQ[:, 512:1024], ones_rf[:, :], rowA[:, tt * 512: tt * 512 + 512],
                    start=True, stop=True), inc=[(bcm2, 1)])
                k.op("scalar", lambda e, tt=tt: e.copy(bA2[:, tt * 512: tt * 512 + 512], pYQ[:, 0:512]),
                     wait=[(bcm2, tt + 1)])
                k.op("scalar", lambda e, tt=tt: e.activation(
                    bB2[:, tt * 512: tt * 512 + 512], pYQ[:, 512:1024], AF.Identity, scale=-1.0),
                     inc=[(bcd2, 2)])
            if FFN_CUT <= 3:
                k.bar(); k.flush(); return
            bb2v = k.sem("bb2v")
            for tt in range(4):
                k.op("vector", lambda e, tt=tt: e.tensor_tensor(
                    bB2[:, tt * 512: tt * 512 + 512], bB2[:, tt * 512: tt * 512 + 512],
                    bA2[:, tt * 512: tt * 512 + 512], OP.mult),
                     wait=[(bcd2, 8)] if tt == 0 else [], inc=[(bb2v, 1)])
            if FFN_CUT <= 4:
                k.bar(); k.flush(); return
            btc = k.sem("btc")
            btm = k.sem("btm")
            for t2 in range(16):
                k.op("tensor", lambda e, t2=t2: e.matmul(
                    pOT[:, 0:1], rowB[:, t2 * 128: t2 * 128 + 128], ones_rf[0:1, 0:1],
                    start=True, stop=True),
                     wait=[(vr2, 3)] if t2 == 0 else [(btc, t2)], inc=[(btm, 1)])
                k.op("scalar", lambda e, t2=t2: e.copy(betac[:, t2:t2 + 1], pOT[:, 0:1]),
                     wait=[(btm, t2 + 1)], inc=[(btc, 1)])
            if FFN_CUT <= 5:
                k.bar(); k.flush(); return
            m2r = k.sem("m2r")
            k.op("scalar", lambda e: e.copy(m2row[:, :], rowA[:, :]), wait=[(vr2, 3)],
                 inc=[(m2r, 1)])
            x1br = k.sem("x1br")
            for kt in range(8):
                k.op("scalar", lambda e, kt=kt: e.activation(
                    x1b[:, kt * N:(kt + 1) * N], X[:, kt * N:(kt + 1) * N], AF.Identity,
                    scale=s1c[:, kt:kt + 1]), wait=[(mmq2, mmq2.n)] if kt == 0 else [],
                     inc=[(x1br, 1)])
            if dbg:
                k.op("sync", lambda e: e.dma_start(dbg_out["d_x1b"], x1b[:, :]),
                     wait=[(x1br, 8)], inc=dma.w_inc())
                k.op("sync", lambda e: e.dma_start(dbg_out["d_r2"], x1b[:, :]),
                     wait=[(x1br, 8)], inc=dma.w_inc())
                k.op("sync", lambda e: e.dma_start(dbg_out["d_s"], X[:, :]),
                     wait=[(x1br, 8)], inc=dma.w_inc())
                k.op("sync", lambda e: e.dma_start(dbg_out["d_hh"][:, 0:8], s1c[:, :]),
                     wait=[(x1br, 8)], inc=dma.w_inc())
            k.bar()

            if FFN_CUT <= 6:
                k.bar(); k.flush(); return
            ccol = k.sem("ccol")

            csm = k.sem("csm")

            def colsum(dst_off, wait0):
                for half in range(2):
                    k.mm(pYQ[0:1, half * 512: half * 512 + 512],
                         [(s1cb[:, kt:kt + 1],
                           warena[:, kt * 1024 + half * 512: kt * 1024 + half * 512 + 512])
                          for kt in range(8)],
                         wait=(wait0 + G["dpair"]() + [(ccol, ccol.n)]) if half == 0 else [],
                         inc=[(csm, 1)])
                    k.op("scalar", lambda e, half=half, dst_off=dst_off: e.activation(
                        negc[:, dst_off + half * 512: dst_off + half * 512 + 512],
                        pYQ[0:1, half * 512: half * 512 + 512], AF.Identity, scale=-1.0),
                         wait=[(csm, csm.n)], inc=[(ccol, 1)])

            colsum(D, [(aw2, awk_t)])

            if FFN_CUT <= 7:
                k.bar(); k.flush(); return
            tab = PP(k, "tab", 2, pinc=32)
            pqf = PP(k, "pqf", 2)
            ptb = PP(k, "ptb", 1)
            stg_pp = PP(k, "stg_pp", 2, pinc=1, cinc=16)
            rope_done = k.sem("rope_done")

            def proj_tile(tp, ncoff, do_rope, dst, dst_wait, wfirst=()):
                if do_rope:
                    tw = tab.widx
                    k.op("gpsimd", lambda e, tp=tp, tw=tw: e.dma_start(
                        tabring[:, tw * 2048: tw * 2048 + 1024],
                        g["cosP"][tp * 128:(tp + 1) * 128, :]),
                         wait=tab.w_wait(), inc=[(tab.prod, 16)])
                    k.op("gpsimd", lambda e, tp=tp, tw=tw: e.dma_start(
                        tabring[:, tw * 2048 + 1024: tw * 2048 + 2048],
                        g["sinP"][tp * 128:(tp + 1) * 128, :]),
                         inc=[(tab.prod, 16)])
                    tab.rw += 1
                for half in range(2):
                    wi = pqf.widx
                    psl = pSc[:, wi * 512: wi * 512 + 512]
                    pairs = [(x1b[:, kt * N + tp * 128: kt * N + tp * 128 + 128],
                              warena[:, kt * 1024 + half * 512: kt * 1024 + half * 512 + 512])
                             for kt in range(8)]
                    pairs.append((m2row[:, tp * 128:(tp + 1) * 128],
                                  negc[:, ncoff + half * 512: ncoff + half * 512 + 512]))
                    w = pqf.w_wait() + (list(wfirst) if half == 0 else [])
                    k.mm(psl, pairs, wait=w, inc=pqf.w_inc())
                    out_ap = fscr[:, half * 512: half * 512 + 512] if do_rope else \
                        dst[:, half * 512: half * 512 + 512]
                    wv = pqf.r_wait()
                    if not do_rope and half == 0:
                        wv = wv + list(dst_wait)
                    k.op("vector", lambda e, o=out_ap, psl=psl, tp=tp: e.tensor_scalar(
                        o, psl, betac[:, tp:tp + 1], None, OP.mult),
                         wait=wv, inc=pqf.r_inc() + ([] if do_rope else [(rope_done, 1)]))
                if do_rope:
                    k.op("vector", lambda e, tw=tw, dst=dst: e.tensor_tensor(
                        dst, fscr[:, 0:1024], tabring[:, tw * 2048: tw * 2048 + 1024], OP.mult),
                         wait=tab.r_wait() + list(dst_wait))
                    swap = bass.AP(fscr, 1, [[4096, 128], [2, 512], [-1, 2]])
                    k.op("vector", lambda e, tw=tw, sw=swap: e.tensor_tensor(
                        fscr[:, 1024:2048], sw,
                        tabring[:, tw * 2048 + 1024: tw * 2048 + 2048], OP.mult))
                    k.op("vector", lambda e, dst=dst: e.tensor_tensor(
                        dst, dst, fscr[:, 1024:2048], OP.add),
                         inc=tab.r_inc() + [(rope_done, 1)])

            ktr = k.sem("ktr")
            for tp in range(16):
                ws = stg_pp.widx
                dst = fscr[:, 2048 + (tp % 2) * 1024: 2048 + (tp % 2) * 1024 + 1024]
                wf = [(ccol, 2), (x1br, 8), (m2r, 1), (btc, 16)] if tp == 0 else []
                proj_tile(tp, D, True, dst, [(ptb.prod, ptb.prod.n)], wf)
                rp_t = rope_done.n
                for dks in range(8):
                    k.op("tensor", lambda e, dks=dks, dst=dst: e.transpose(
                        pTb[:, dks * 128: dks * 128 + 128], dst[:, dks * 128: dks * 128 + 128],
                        identb[:, :]),
                         wait=(ptb.w_wait() + [(rope_done, rp_t)]) if dks == 0 else [],
                         inc=ptb.w_inc() if dks == 7 else [])
                for dks in range(8):
                    k.op("scalar", lambda e, dks=dks, ws=ws: e.copy(
                        stg[:, ws * 1024 + dks * 128: ws * 1024 + dks * 128 + 128],
                        pTb[:, dks * 128: dks * 128 + 128]),
                         wait=(ptb.r_wait() + stg_pp.w_wait()) if dks == 0 else [],
                         inc=(ptb.r_inc() + [(ktr, 1)]) if dks == 7 else [])
                stg_pp.rw += 1
                k.op("scalar", k.dummy["scalar"], inc=[(stg_pp.prod, 1)])
                k.op("sync", lambda e, tp=tp, ws=ws: e.dma_start(
                    bass.AP(kT_scr.tensor, tp * 128, [[N, 128], [N * 128, 8], [1, 128]]),
                    stg[:, ws * 1024: ws * 1024 + 1024].rearrange("p (dks c) -> p dks c", c=128)),
                     wait=stg_pp.r_wait(), inc=stg_pp.r_inc())
            if FFN_CUT <= 8:
                k.bar(); k.flush(); return
            for kt in range(8):
                k.op("gpsimd", lambda e, kt=kt: e.dma_start(
                    warena[:, kt * 1024:(kt + 1) * 1024], g["wv"][kt * 128:(kt + 1) * 128, :]),
                     wait=[(ktr, 16)] if kt == 0 else [], inc=[(aw2, 16)])
            colsum(2 * D, [(aw2, aw2.n)])
            for tp in range(16):
                ws = stg_pp.widx
                dst = stg[:, ws * 1024: ws * 1024 + 1024]
                proj_tile(tp, 2 * D, False, dst, stg_pp.w_wait(),
                          [(ccol, 4)] if tp == 0 else [])
                stg_pp.rw += 1
                k.op("vector", k.dummy["vector"], inc=[(stg_pp.prod, 1)])
                k.op("sync", lambda e, tp=tp, dst=dst: e.dma_start(
                    v_scr[tp * 128:(tp + 1) * 128, :], dst),
                     wait=stg_pp.r_wait(), inc=stg_pp.r_inc())
            if FFN_CUT <= 9:
                k.bar(); k.flush(); return
            for kt in range(8):
                k.op("gpsimd", lambda e, kt=kt: e.dma_start(
                    warena[:, kt * 1024:(kt + 1) * 1024], g["wq_r"][kt * 128:(kt + 1) * 128, :]),
                     wait=[(rope_done, rope_done.n)] if kt == 0 else [], inc=[(aw2, 16)])
            colsum(0, [(aw2, aw2.n)])
            for tp in range(16):
                ws = stg_pp.widx
                dst = stg[:, ws * 1024: ws * 1024 + 1024]
                proj_tile(tp, 0, True, dst, stg_pp.w_wait(), [(ccol, 6)] if tp == 0 else [])
                stg_pp.rw += 1
                k.op("vector", k.dummy["vector"], inc=[(stg_pp.prod, 1)])
                k.op("sync", lambda e, tp=tp, dst=dst: e.dma_start(
                    q_scr[tp * 128:(tp + 1) * 128, :], dst),
                     wait=stg_pp.r_wait(), inc=stg_pp.r_inc())
            if FFN_CUT <= 10:
                k.bar(); k.flush(); return
            for kt in range(8):
                k.op("gpsimd", lambda e, kt=kt: e.dma_start(
                    warena[:, kt * 1024:(kt + 1) * 1024], g["wo"][kt * 128:(kt + 1) * 128, :]),
                     wait=[(rope_done, rope_done.n)] if kt == 0 else [], inc=[(aw2, 16)])
            awo_t = aw2.n
            qspill_t = stg_pp.cons.n
            k.bar()
            k.flush()

        es2b = ExitStack()
        with es2b:
            sb2b = lambda n_, p, f_, dt: es2b.enter_context(nc.sbuf_tensor(n_, [p, f_], dt))
            k_T = sb2b("k_T", 128, 8 * N, BF16)
            v_sb = sb2b("v_sb", 128, 16 * D, BF16)
            stair_s = sb2b("stair_s", 128, 2048, BF16)
            exp_s = sb2b("exp_s", 128, 2048, BF16)
            exps_n = sb2b("exps_n", 128, 2048, BF16)
            expT = sb2b("expT", 128, 2048, BF16)
            qld = sb2b("qld", 128, 2 * 1024, BF16)
            q_T_sb = sb2b("q_T_sb", 128, 1024, BF16)
            oT_sb = sb2b("oT_sb", 128, 1024, BF16)
            y_T = sb2b("y_T", 128, 1024, BF16)
            hh = sb2b("hh", 128, 1024, F32)
            r2blk = sb2b("r2blk", 128, 2 * 1024, BF16)
            vf32 = sb2b("vf32", 128, 1024, F32)

            load(stair_s[:, :], g["stairS"], "gpsimd")
            load(k_T[:, :].rearrange("p (kt n) -> p kt n", n=N),
                 kT_scr.rearrange("(kt p) n -> p kt n", p=128))
            load(v_sb[:, :].rearrange("p (tp c) -> p tp c", c=D),
                 v_scr.rearrange("(tp p) c -> p tp c", p=128))
            attn_pair = G["dpair"]()

            qrl = PP(k, "qrl", 2, pinc=16)
            psc = PP(k, "psc", 1, cinc=2)
            ptb2 = PP(k, "ptb2", 1)
            pot = PP(k, "pot", 1)
            pyq = PP(k, "pyq", 2)
            r2sp = PP(k, "r2sp", 2, pinc=1, cinc=16)
            trh = k.sem("trh")
            smax = k.sem("smax")
            expd = k.sem("expd")
            expn = k.sem("expn")
            oTev = k.sem("oTev")
            ytok = k.sem("ytok")
            epd = k.sem("epd")
            gsq = k.sem("gsq")
            r2v = k.sem("r2v")
            xs2 = k.sem("xs2")

            for qt in range(FFN_QT):
                kbanks = qt // 4 + 1
                nkt = qt + 1
                wi_q = qrl.widx
                k.op("sync", lambda e, qt=qt, wi=wi_q: e.dma_start(
                    qld[:, wi * 1024: wi * 1024 + 1024], q_scr[qt * 128:(qt + 1) * 128, :]),
                     wait=qrl.w_wait(), inc=qrl.w_inc())
                for dks in range(8):
                    k.op("tensor", lambda e, dks=dks, wi=wi_q: e.transpose(
                        pTb[:, dks * 128: dks * 128 + 128],
                        qld[:, wi * 1024 + dks * 128: wi * 1024 + dks * 128 + 128], identb[:, :]),
                         wait=(ptb2.w_wait() + qrl.r_wait() + (attn_pair if qt == 0 else []))
                         if dks == 0 else [],
                         inc=(ptb2.w_inc() + qrl.r_inc()) if dks == 7 else [])
                k.op("scalar", lambda e: e.copy(q_T_sb[:, :], pTb[:, :]),
                     wait=ptb2.r_wait() + [(ytok, ytok.n)],
                     inc=ptb2.r_inc() + [(k.sem("qTr"), 1)])
                qT_t = k.sem("qTr").n

                for h in range(HF):
                    pt_h = h // 2
                    off = (h % 2) * 64
                    for kb in range(kbanks):
                        diag = (kb == kbanks - 1)
                        k.op("tensor", lambda e, kb=kb, pt_h=pt_h, off=off, diag=diag: e.matmul(
                            pSc[:, kb * 512: kb * 512 + 512],
                            q_T_sb[off:off + 64, pt_h * 128: pt_h * 128 + 128],
                            k_T[off:off + 64, pt_h * N + kb * 512: pt_h * N + kb * 512 + 512],
                            start=True, stop=not diag),
                             wait=(psc.w_wait() + [(k.sem("qTr"), qT_t)]) if kb == 0 else [])
                    k.op("tensor", lambda e, qt=qt, kbanks=kbanks: e.matmul(
                        pSc[:, (kbanks - 1) * 512: kbanks * 512], identb[:, :],
                        stair_s[:, (qt % 4) * 512: (qt % 4) * 512 + 512], start=False, stop=True),
                         inc=psc.w_inc())
                    k.op("vector", lambda e, kb4=kbanks * 512: e.tensor_scalar(
                        exp_s[:, 0:kb4], pSc[:, 0:kb4], 0.125, None, OP.mult, OP.max,
                        accum_out=mxden[:, 0:1]), wait=psc.r_wait(), inc=[(k.sem("nsm"), 1)])
                    k.op("vector", lambda e: e.tensor_scalar(
                        mxden[:, 1:2], mxden[:, 0:1], -1.0, None, OP.mult),
                         wait=[(k.sem("nsm"), k.sem("nsm").n)], inc=[(smax, 1)])
                    sm_t = smax.n
                    for kb in range(kbanks):
                        k.op("scalar", lambda e, kb=kb: e.activation(
                            exp_s[:, kb * 512: kb * 512 + 512], pSc[:, kb * 512: kb * 512 + 512],
                            AF.Exp, bias=mxden[:, 1:2], scale=0.125,
                            accum_out=mxden[:, 2 + kb: 3 + kb]),
                             wait=[(smax, sm_t)] if kb == 0 else [],
                             inc=[(expd, 1)] if kb == kbanks - 1 else [])
                    ex_t = expd.n
                    k.op("vector", lambda e, kbanks=kbanks: e.tensor_scalar(
                        vf32[:, 0:kbanks], mxden[:, 2:2 + kbanks], 1.0, None, OP.mult, OP.add,
                        accum_out=mxden[:, 8:9]), wait=[(expd, ex_t)], inc=[(k.sem("dnv"), 1)])
                    k.op("vector", lambda e: e.reciprocal(mxden[:, 9:10], mxden[:, 8:9]),
                         wait=[(k.sem("dnv"), k.sem("dnv").n)], inc=[(k.sem("dnv"), 1)])
                    k.op("vector", lambda e, kb4=kbanks * 512: e.tensor_scalar(
                        exps_n[:, 0:kb4], exp_s[:, 0:kb4], mxden[:, 9:10], None, OP.mult),
                         wait=[(trh, trh.n), (k.sem("dnv"), k.sem("dnv").n)],
                         inc=[(expn, 1)] + psc.r_inc())
                    en_t = expn.n
                    exev = k.sem("exev")
                    for base in range(0, nkt, 8):
                        hi = min(base + 8, nkt)
                        for kt2 in range(base, hi):
                            k.op("tensor", lambda e, kt2=kt2, base=base: e.transpose(
                                pTb[:, (kt2 - base) * 128: (kt2 - base) * 128 + 128],
                                exps_n[:, kt2 * 128: kt2 * 128 + 128], identb[:, :]),
                                 wait=(ptb2.w_wait() + [(expn, en_t)]) if kt2 == base else [],
                                 inc=(ptb2.w_inc() + ([(trh, 1)] if hi == nkt else []))
                                 if kt2 == hi - 1 else [])
                        for kt2 in range(base, hi):
                            w = []
                            if kt2 == base:
                                w = ptb2.r_wait()
                                if base == 0:
                                    w = w + [(pot.prod, pot.prod.n)]
                            k.op("scalar", lambda e, kt2=kt2, base=base: e.copy(
                                expT[:, kt2 * 128: kt2 * 128 + 128],
                                pTb[:, (kt2 - base) * 128: (kt2 - base) * 128 + 128]),
                                 wait=w,
                                 inc=([(exev, 1)] + (ptb2.r_inc() if kt2 == hi - 1 else [])))
                    k.mm(pOT[0:64, 0:128],
                         [(v_sb[:, kt2 * D + h * 64: kt2 * D + h * 64 + 64],
                           expT[:, kt2 * 128: kt2 * 128 + 128]) for kt2 in range(nkt)],
                         wait=[(exev, exev.n)] + pot.w_wait(), inc=pot.w_inc())
                    k.op("scalar", lambda e, off=off, pt_h=pt_h: e.copy(
                        oT_sb[off:off + 64, pt_h * 128: pt_h * 128 + 128], pOT[0:64, 0:128]),
                         wait=pot.r_wait(), inc=pot.r_inc() + [(oTev, 1)])
                oT_t = oTev.n
                for half in range(2):
                    wi = pyq.widx
                    psl = pYQ[:, wi * 512: wi * 512 + 512]
                    k.mm(psl, [(oT_sb[:, p2 * 128: p2 * 128 + 128],
                                warena[:, p2 * 1024 + half * 512: p2 * 1024 + half * 512 + 512])
                               for p2 in range(8)],
                         wait=pyq.w_wait() + ([(oTev, oT_t)] if half == 0 else []),
                         inc=pyq.w_inc())
                    k.op("scalar", lambda e, half=half, psl=psl: e.copy(
                        fscr[:, half * 512: half * 512 + 512], psl),
                         wait=pyq.r_wait(), inc=pyq.r_inc() + [(ytok, 1)])
                yt_t = ytok.n
                for dks in range(8):
                    k.op("tensor", lambda e, dks=dks: e.transpose(
                        pTb[:, dks * 128: dks * 128 + 128], fscr[:, dks * 128: dks * 128 + 128],
                        identb[:, :]),
                         wait=(ptb2.w_wait() + [(ytok, yt_t), (epd, max(0, qt - 1))]) if dks == 0 else [],
                         inc=ptb2.w_inc() if dks == 7 else [])
                for dks in range(8):
                    k.op("scalar", lambda e, dks=dks: e.copy(
                        y_T[:, dks * 128: dks * 128 + 128], pTb[:, dks * 128: dks * 128 + 128]),
                         wait=ptb2.r_wait() if dks == 0 else [],
                         inc=(ptb2.r_inc() + [(k.sem("yTr"), 1)]) if dks == 7 else [])
                yTr_t = k.sem("yTr").n

                for dm in range(8):
                    xsl = X[:, dm * N + qt * 128: dm * N + qt * 128 + 128]
                    k.op("vector", lambda e, qt=qt, xsl=xsl: e.tensor_tensor(
                        vf32[:, 0:128], xsl, bA2[:, qt * 128: qt * 128 + 128], OP.mult),
                         wait=[(k.sem("yTr"), yTr_t), (bb2v, 4), (gsq, gsq.n)] if dm == 0 else [])
                    k.op("vector", lambda e, dm=dm, qt=qt: e.scalar_tensor_tensor(
                        vf32[:, 128:256], y_T[:, dm * 128: dm * 128 + 128], 1.0,
                        bB2[:, qt * 128: qt * 128 + 128], OP.mult, OP.add))
                    k.op("vector", lambda e, dm=dm: e.tensor_tensor(
                        hh[:, dm * 128: dm * 128 + 128], vf32[:, 0:128], vf32[:, 128:256], OP.add),
                         inc=[(k.sem("hhv"), 1)])
                hh_t = k.sem("hhv").n
                for dm in range(8):
                    k.op("scalar", lambda e, dm=dm: e.activation(
                        fscr[:, 1024 + dm * 128: 1024 + dm * 128 + 128],
                        hh[:, dm * 128: dm * 128 + 128], AF.Square),
                         wait=[(k.sem("hhv"), hh_t)] if dm == 0 else [], inc=[(gsq, 1)])
                wi_g = pyq.widx
                pg1 = pYQ[0:1, wi_g * 512: wi_g * 512 + 128]
                k.mm(pg1, [(ones_col[:, :], fscr[:, 1024 + dm * 128: 1024 + dm * 128 + 128])
                           for dm in range(8)],
                     wait=[(gsq, gsq.n)] + pyq.w_wait(), inc=pyq.w_inc())
                k.op("scalar", lambda e, pg1=pg1: e.activation(
                    grow[:, :], pg1, AF.Ln, scale=1.0 / D, bias=cb[0:1, 2:3]), wait=pyq.r_wait())
                k.op("scalar", lambda e: e.activation(grow[:, :], grow[:, :], AF.Exp, scale=-0.5),
                     inc=pyq.r_inc() + [(k.sem("grw"), 1)])
                wi_g = pyq.widx
                pg2 = pYQ[:, wi_g * 512: wi_g * 512 + 128]
                k.op("tensor", lambda e, pg2=pg2: e.matmul(
                    pg2, ones_rf[:, :], grow[:, :], start=True, stop=True),
                     wait=[(k.sem("grw"), k.sem("grw").n)] + pyq.w_wait(), inc=pyq.w_inc())
                k.op("scalar", lambda e, pg2=pg2: e.copy(Gb[:, :], pg2),
                     wait=pyq.r_wait() + [(r2v, r2v.n)], inc=pyq.r_inc() + [(k.sem("gbc"), 1)])
                gb_t = k.sem("gbc").n
                wb = r2sp.widx
                for dm in range(8):
                    xsl = X[:, dm * N + qt * 128: dm * N + qt * 128 + 128]
                    k.op("vector", lambda e, dm=dm, wb=wb: e.scalar_tensor_tensor(
                        r2blk[:, wb * 1024 + dm * 128: wb * 1024 + dm * 128 + 128],
                        hh[:, dm * 128: dm * 128 + 128], s2c[:, dm:dm + 1], Gb[:, :],
                        OP.mult, OP.mult),
                         wait=([(k.sem("gbc"), gb_t)] + r2sp.w_wait()) if dm == 0 else [],
                         inc=[(r2v, 1)])
                    k.op("vector", lambda e, xsl=xsl, dm=dm: e.tensor_tensor(
                        xsl, xsl, hh[:, dm * 128: dm * 128 + 128], OP.add),
                         inc=[(xs2, 1)] + (([(epd, 1), (k.sem("r2w"), 1)]) if dm == 7 else []))
                r2sp.rw += 1
                k.op("sync", lambda e, qt=qt, wb=wb: e.dma_start(
                    r2_scr.rearrange("(kt p) n -> p kt n", p=128)[:, :, qt * 128:(qt + 1) * 128],
                    r2blk[:, wb * 1024: wb * 1024 + 1024].rearrange("p (kt c) -> p kt c", c=128)),
                     wait=[(k.sem("r2w"), k.sem("r2w").n)], inc=r2sp.r_inc())
                if dbg and qt == 0:
                    k.op("sync", lambda e: e.dma_start(dbg_out["d_hh"], hh[:, :]),
                         wait=[(k.sem("hhv"), hh_t)], inc=dma.w_inc())
                    k.op("vector", lambda e: e.nop(), wait=G["dpair"]())
            if dbg:
                k.op("sync", lambda e: e.dma_start(dbg_out["d_kT"], k_T[:, :]), inc=dma.w_inc())
                k.op("sync", lambda e: e.dma_start(dbg_out["d_vsb"], v_sb[:, :]), inc=dma.w_inc())
                k.op("sync", lambda e: e.dma_start(dbg_out["d_s"], X[:, :]),
                     wait=[(xs2, xs2.n)], inc=dma.w_inc())
                k.op("vector", lambda e: e.nop(), wait=G["dpair"]())
            k.bar()
            k.flush()


def _moe(k, nc, g, G, dma, load, dbg, dbg_out, r2_scr, out):
    X = G["X"]
    identb = G["identb"]
    ones_rb = G["ones_rb"]
    es3 = ExitStack()
    with es3:
        sb3 = lambda n_, p, f_, dt: es3.enter_context(nc.sbuf_tensor(n_, [p, f_], dt))
        ps3 = lambda n_, p, f_, dt: es3.enter_context(nc.psum_tensor(n_, [p, f_], dt))

        r2s = sb3("r2s", 128, 8 * N, BF16)
        w12 = sb3("w12", 128, 16384, BF16)
        woutr = sb3("woutr", 128, 4 * 1024, BF16)
        h_ew = sb3("h_ew", 128, 8 * N, BF16)
        u_all = sb3("u_all", 8, N, BF16)
        u_b = sb3("u_b", 128, N, BF16)
        oh8 = sb3("oh8", 8, 1024, BF16)
        noise_s = sb3("noise_s", 128, 128, F32)
        b1c_s = sb3("b1c_s", 128, 64, F32)
        b2c_s = sb3("b2c_s", 128, 64, F32)
        bout_s = sb3("bout_s", 8, D, BF16)
        router_s = sb3("router_s", 128, 64, BF16)
        mscr = sb3("mscr", 128, 2048, BF16)
        mf32 = sb3("mf32", 128, 64, F32)

        pR = ps3("pR", 128, 512, F32)
        pAm = ps3("pAm", 128, 1024, F32)
        pBm = ps3("pBm", 128, 1024, F32)
        pEo = ps3("pEo", 128, 1024, F32)
        pTm = ps3("pTm", 8, 128, BF16)

        load(r2s[:, :].rearrange("p (kt n) -> p kt n", n=N),
             r2_scr.rearrange("(kt p) n -> p kt n", p=128))
        load(noise_s[:, :], g["noise"])
        load(b1c_s[:, :], g["b1c"])
        load(b2c_s[:, :], g["b2c"])
        load(bout_s[:, :], g["bout"], "gpsimd")
        load(router_s[:, :], g["router_c"], "gpsimd")
        load(oh8[:, :], g["onehot8"], "gpsimd")
        moe_pair = G["dpair"]()
        if dbg:
            k.op("sync", lambda e: e.dma_start(dbg_out["d_r2"], r2s[:, :]),
                 wait=moe_pair, inc=dma.w_inc())

        prp = PP(k, "prp", 1)
        ptm = PP(k, "ptm", 1)
        vstage = k.sem("vstage")
        sstage = k.sem("sstage")
        uready = k.sem("uready")
        utr = k.sem("utr")
        uall = k.sem("uall")
        for tp in range(16):
            s = tp % 2
            m = mf32[:, s * 32: s * 32 + 32]
            k.mm(pR[:, 0:8],
                 [(r2s[:, kt * N + tp * 128: kt * N + tp * 128 + 128],
                   router_s[:, kt * 8:(kt + 1) * 8]) for kt in range(8)],
                 wait=prp.w_wait() + (moe_pair if tp == 0 else []),
                 inc=prp.w_inc())
            k.op("vector", lambda e, m=m, tp=tp: e.tensor_tensor(
                m[:, 0:8], pR[:, 0:8], noise_s[:, tp * 8:(tp + 1) * 8], OP.add),
                 wait=prp.r_wait() + [(sstage, max(0, tp - 1))], inc=prp.r_inc())
            k.op("vector", lambda e, m=m: e.tensor_reduce(
                m[:, 17:18], m[:, 0:8], AX, OP.max, negate=True), inc=[(vstage, 1)])
            k.op("scalar", lambda e, m=m: e.activation(
                m[:, 0:8], m[:, 0:8], AF.Exp, bias=m[:, 17:18], accum_out=m[:, 19:20]),
                 wait=[(vstage, vstage.n)], inc=[(k.sem("sse"), 1)])
            k.op("scalar", lambda e, m=m: e.activation(m[:, 20:21], m[:, 19:20], AF.Ln),
                 wait=[(k.sem("sse"), k.sem("sse").n)])
            k.op("scalar", lambda e, m=m: e.activation(
                m[:, 20:21], m[:, 20:21], AF.Exp, scale=-1.0), inc=[(sstage, 1)])
            ve = k.sem("ve")
            k.op("vector", lambda e, m=m: e.tensor_reduce(
                m[:, 18:19], m[:, 0:8], AX, OP.max),
                 wait=[(sstage, sstage.n)], inc=[(ve, 1)])
            k.op("vector", lambda e, m=m: e.tensor_scalar(
                m[:, 8:16], m[:, 0:8], m[:, 18:19], None, OP.is_equal),
                 wait=[(ve, ve.n)], inc=[(ve, 1)])
            k.op("vector", lambda e, m=m: e.scalar_tensor_tensor(
                m[:, 8:16], m[:, 8:16], -1e30, m[:, 0:8], OP.mult, OP.add),
                 wait=[(ve, ve.n)], inc=[(ve, 1)])
            k.op("vector", lambda e, m=m: e.tensor_reduce(
                m[:, 21:22], m[:, 8:16], AX, OP.max),
                 wait=[(ve, ve.n)], inc=[(ve, 1)])
            k.op("vector", lambda e, m=m: e.tensor_scalar(
                m[:, 8:16], m[:, 0:8], m[:, 21:22], None, OP.is_ge),
                 wait=[(ve, ve.n)], inc=[(ve, 1)])
            k.op("vector", lambda e, m=m: e.tensor_tensor(
                m[:, 8:16], m[:, 8:16], m[:, 0:8], OP.mult),
                 wait=[(ve, ve.n)], inc=[(ve, 1)])
            k.op("vector", lambda e, m=m, s=s: e.tensor_scalar(
                mscr[:, s * 8: s * 8 + 8], m[:, 8:16], m[:, 20:21], None, OP.mult),
                 wait=[(utr, max(0, tp - 1)), (ve, ve.n)], inc=[(uready, 1)])
            k.op("tensor", lambda e, s=s: e.transpose(
                pTm[:, 0:128], mscr[:, s * 8: s * 8 + 8], identb[:, :]),
                 wait=[(uready, uready.n)] + ptm.w_wait(), inc=ptm.w_inc() + [(utr, 1)])
            k.op("scalar", lambda e, tp=tp: e.copy(
                u_all[:, tp * 128:(tp + 1) * 128], pTm[0:8, 0:128]),
                 wait=ptm.r_wait(), inc=ptm.r_inc() + [(uall, 1)])
        if dbg:
            k.op("sync", lambda e: e.dma_start(dbg_out["d_u"], u_all[:, :]),
                 wait=[(uall, 16)], inc=dma.w_inc())
            k.op("sync", lambda e: e.dma_start(dbg_out["d_cl"][:, 0:64], mf32[:, :]),
                 wait=[(uready, 16)], inc=dma.w_inc())
        k.bar()

        w12d = k.sem("w12d")
        w12u = k.sem("w12u")
        ubr = k.sem("ubr")
        heAc = k.sem("heAc")
        eomm = k.sem("eomm")
        xmoe = k.sem("xmoe")
        pub = PP(k, "pub", 1)
        pam = PP(k, "pam", 2)
        pbm = PP(k, "pbm", 2)
        pms = PP(k, "pms", 2)
        pwo = PP(k, "pwo", 4, pinc=16)
        peo = PP(k, "peo", 2)
        for eix in range(MOE_EXP):
            for half, wname in [(0, "w1"), (1, "w2")]:
                k.op("gpsimd", lambda e, eix=eix, half=half, wname=wname: e.dma_start(
                    w12[:, half * 8192: half * 8192 + 8192].rearrange("p (kt c) -> p kt c", c=1024),
                    g[wname][eix * D:(eix + 1) * D, :].rearrange("(kt p) c -> p kt c", p=128)),
                     wait=[(w12u, eix)] if half == 0 else [], inc=[(w12d, 16)])
            w12_t = w12d.n
            for tt in range(4):
                k.op("tensor", lambda e, eix=eix, tt=tt: e.matmul(
                    pR[:, 0:512], oh8[:, eix * 128:(eix + 1) * 128],
                    u_all[:, tt * 512:(tt + 1) * 512], start=True, stop=True),
                     wait=pub.w_wait() + ([(uall, 16), (heAc, 32 * eix)] if tt == 0 else []),
                     inc=pub.w_inc())
                k.op("scalar", lambda e, tt=tt: e.copy(u_b[:, tt * 512:(tt + 1) * 512], pR[:, 0:512]),
                     wait=pub.r_wait(), inc=pub.r_inc() + [(ubr, 1)])
            ub_t = ubr.n
            for mt in range(8):
                for tt in range(4):
                    wa = pam.widx
                    pasl = pAm[:, wa * 512: wa * 512 + 512]
                    w = pam.w_wait()
                    if mt == 0 and tt == 0:
                        w = w + [(w12d, w12_t)] + moe_pair
                    k.mm(pasl, [(w12[:, kt * 1024 + mt * 128: kt * 1024 + mt * 128 + 128],
                                 r2s[:, kt * N + tt * 512: kt * N + tt * 512 + 512])
                                for kt in range(8)], wait=w, inc=pam.w_inc())
                    wb2 = pbm.widx
                    pbsl = pBm[:, wb2 * 512: wb2 * 512 + 512]
                    inc = pbm.w_inc()
                    if mt == 7 and tt == 3:
                        inc = inc + [(w12u, 1)]
                    k.mm(pbsl, [(w12[:, 8192 + kt * 1024 + mt * 128: 8192 + kt * 1024 + mt * 128 + 128],
                                 r2s[:, kt * N + tt * 512: kt * N + tt * 512 + 512])
                                for kt in range(8)], wait=pbm.w_wait(), inc=inc)
                    s = pms.widx
                    k.op("scalar", lambda e, mt=mt, eix=eix, pasl=pasl, s=s: e.activation(
                        mscr[:, s * 1024: s * 1024 + 512], pasl, AF.Silu,
                        bias=b1c_s[:, mt * 8 + eix: mt * 8 + eix + 1]),
                         wait=pam.r_wait() + pms.w_wait(), inc=pam.r_inc())
                    k.op("scalar", lambda e, mt=mt, eix=eix, pbsl=pbsl, s=s: e.activation(
                        mscr[:, s * 1024 + 512: s * 1024 + 1024], pbsl, AF.Identity,
                        bias=b2c_s[:, mt * 8 + eix: mt * 8 + eix + 1]),
                         wait=pbm.r_wait(), inc=pbm.r_inc() + pms.w_inc())
                    w = pms.r_wait() + [(ubr, ub_t)]
                    if mt == 0 and tt == 0:
                        w = w + [(eomm, 32 * eix)]
                    k.op("vector", lambda e, s=s: e.tensor_tensor(
                        mscr[:, s * 1024: s * 1024 + 512], mscr[:, s * 1024: s * 1024 + 512],
                        mscr[:, s * 1024 + 512: s * 1024 + 1024], OP.mult), wait=w)
                    k.op("vector", lambda e, mt=mt, tt=tt, s=s: e.tensor_tensor(
                        h_ew[:, mt * N + tt * 512: mt * N + tt * 512 + 512],
                        mscr[:, s * 1024: s * 1024 + 512], u_b[:, tt * 512:(tt + 1) * 512], OP.mult),
                         inc=pms.r_inc() + [(heAc, 1)])
            he_t = heAc.n
            for dm in range(8):
                wo_i = pwo.widx
                k.op("gpsimd", lambda e, eix=eix, dm=dm, wo_i=wo_i: e.dma_start(
                    woutr[:, wo_i * 1024: wo_i * 1024 + 1024].rearrange("p (kt c) -> p kt c", c=128),
                    g["wout"][eix * D:(eix + 1) * D, dm * 128:(dm + 1) * 128]
                    .rearrange("(kt p) c -> p kt c", p=128)),
                     wait=pwo.w_wait(), inc=pwo.w_inc())
                for tt in range(4):
                    we = peo.widx
                    pesl = pEo[:, we * 512: we * 512 + 512]
                    pairs = []
                    if eix == 0:
                        pairs.append((bout_s[:, dm * 128:(dm + 1) * 128],
                                      u_all[:, tt * 512:(tt + 1) * 512]))
                    pairs += [(woutr[:, wo_i * 1024 + kt * 128: wo_i * 1024 + kt * 128 + 128],
                               h_ew[:, kt * N + tt * 512: kt * N + tt * 512 + 512])
                              for kt in range(8)]
                    w = peo.w_wait()
                    if tt == 0:
                        w = w + pwo.r_wait()
                        if dm == 0:
                            w = w + [(heAc, he_t)]
                    inc = peo.w_inc() + [(eomm, 1)]
                    if tt == 3:
                        inc = inc + pwo.r_inc()
                    k.mm(pesl, pairs, wait=w, inc=inc)
                    k.op("vector", lambda e, dm=dm, tt=tt, pesl=pesl: e.tensor_tensor(
                        X[:, dm * N + tt * 512: dm * N + tt * 512 + 512],
                        X[:, dm * N + tt * 512: dm * N + tt * 512 + 512], pesl, OP.add),
                         wait=peo.r_wait(), inc=peo.r_inc() + [(xmoe, 1)])

        outd = k.sem("outd")
        xm_t = xmoe.n
        for dm in range(8):
            k.op("sync", lambda e, dm=dm: e.dma_start(
                out[dm * 128:(dm + 1) * 128, :], X[:, dm * N:(dm + 1) * N]),
                 wait=[(xmoe, xm_t)] if dm == 0 else [], inc=[(outd, 16)])
        k.op("sync", lambda e: e.nop(), wait=[(outd, 128)])
        k.flush()


def kernel(**inputs):
    d = _host_prep(inputs)
    nc, _ = build(dbg=BUILD_DEBUG)
    x = np.asarray(inputs["L0_attn_x"], np.float32)
    oh = np.zeros((8, 1024), np.float32)
    for e in range(8):
        oh[e, e * 128:(e + 1) * 128] = 1.0
    shared = {nm: v for nm, v in d.items() if not nm.startswith("_")}
    shared["onehot8"] = oh
    in_maps = []
    for b in range(NB):
        m = dict(shared)
        m["xT"] = np.ascontiguousarray(x[b].T)
        m["noise"] = np.ascontiguousarray(
            d["_noise_all"][b].reshape(16, 128, 8).transpose(1, 0, 2).reshape(128, 128))
        in_maps.append(m)
    res = run_bass_kernel_spmd(nc, in_maps, core_ids=list(range(NB)))
    outs = [np.asarray(r["out"], np.float32).T for r in res.results]
    return np.stack(outs, 0)
